# version 03 (final)
# baseline (speedup 1.0000x reference)
import numpy as np
import ml_dtypes
from contextlib import ExitStack

import concourse.bass as bass
import concourse.mybir as mybir
from concourse.bass_utils import run_bass_kernel_spmd

BF16 = ml_dtypes.bfloat16
F32 = mybir.dt.float32
BF = mybir.dt.bfloat16
AF = mybir.ActivationFunctionType
ALU = mybir.AluOpType
AX = mybir.AxisListType

DEBUG_TAPS = {}

NC = 8
T = 8192
D = 2048
R = T // NC
RT = R // 128
CH = D // 128
LN_EPS = 1e-5
RMS_EPS = 1.1920928955078125e-07


def _rows_of_core(c):
    return np.arange(c, 64, NC)

def _shard_xT(x, c):
    blocks = x.reshape(64, 128, D)[_rows_of_core(c)]
    return np.ascontiguousarray(blocks.reshape(R, D).T)

def _unshard_y(ys):
    out = np.empty((64, 128, D), np.float32)
    for c, y in enumerate(ys):
        out[_rows_of_core(c)] = np.ascontiguousarray(np.asarray(y).T).reshape(RT, 128, D)
    return out.reshape(1, T, D)

def _perm_rope_cols(n_heads, rot):
    cols = []
    for h in range(n_heads):
        base = h * rot
        cols.extend(base + np.arange(0, rot, 2))
        cols.extend(base + np.arange(1, rot, 2))
    return np.array(cols)

def _rope_tables(c):
    dim = 64
    inv_freq = 1.0 / (10000.0 ** (np.arange(0, dim, 2, dtype=np.float32) / dim))
    pos = np.arange(T, dtype=np.float32).reshape(64, 128)[_rows_of_core(c)].reshape(-1)
    ang = pos[None, :] * inv_freq[:, None]
    return np.ascontiguousarray(np.cos(ang).astype(np.float32)), \
           np.ascontiguousarray(np.sin(ang).astype(np.float32))

def _rope_tables_rm(c):
    cos, sin = _rope_tables(c)
    def rm(t):
        o = t.T.reshape(RT, 128, 32).transpose(1, 0, 2).reshape(128, RT * 32)
        return np.ascontiguousarray(o)
    return rm(cos), rm(sin)

def _bias_col(b):
    b = np.asarray(b, np.float32)
    n = b.shape[0]
    return np.ascontiguousarray(b.reshape(n // 128, 128).T)

def _mask0(c):
    u = np.arange(2)[:, None, None, None]
    r = np.arange(8)[None, :, None, None]
    p = np.arange(128)[None, None, :, None]
    qq = np.arange(256)[None, None, None, :]
    keep = (128 * (8 * u + r) + p) <= (128 * (8 * (qq // 128) + c) + (qq % 128))
    m = keep.astype(np.float32).transpose(2, 0, 1, 3).reshape(128, 16 * 256)
    return np.ascontiguousarray(m.astype(BF16))

CONST_SHAPES = {
    "cblk": [128, 2], "cblkT": [2, 128], "ones_col": [128, 1],
    "ones_row": [1, 128], "neg_ones_row": [1, 128],
}

def _consts(c):
    cblk = np.zeros((128, 2), np.float32)
    cblk[0:64, 0] = 1.0
    cblk[64:128, 1] = 1.0
    return {
        "cblk": cblk,
        "cblkT": np.ascontiguousarray(cblk.T),
        "ones_col": np.ones((128, 1), np.float32),
        "ones_row": np.ones((1, 128), np.float32),
        "neg_ones_row": -np.ones((1, 128), np.float32),
    }


ENGN = ("tensor", "vector", "scalar", "gpsimd", "sync")

class Stream:
    def __init__(self, p, name):
        self.p = p
        self.name = name
        self.ops = []

    def wait(self, sem, n=None):
        cnt = self.p.val(sem) if n is None else n
        if cnt <= 0:
            return
        h = self.p.sem[sem][0]
        self.ops.append(lambda e, h=h, cnt=cnt: e.wait_ge(h, cnt))

    def op(self, fn, inc=None, inc_n=1):
        if inc is None:
            self.ops.append(fn)
        else:
            h = self.p.sem[inc][0]
            self.p.sem[inc][1] += inc_n
            def g(e, fn=fn, h=h, inc_n=inc_n):
                fn(e).then_inc(h, inc_n)
            self.ops.append(g)

class Arena:
    def __init__(self, t, ncols):
        self.t = t
        self.ncols = ncols
        self.off = 0
        self.marks = []
        self.peak = 0

    def alloc(self, cols, align=16):
        o = (self.off + align - 1) // align * align
        assert o + cols <= self.ncols, f"arena overflow {o}+{cols} > {self.ncols}"
        self.off = o + cols
        self.peak = max(self.peak, self.off)
        return self.t[:, o:o + cols]

    def mark(self):
        self.marks.append(self.off)

    def reset(self):
        self.off = self.marks.pop()

class Prog:
    def __init__(self, nc, st, abf_cols=90600, af32_cols=6200):
        self.nc = nc
        self.st = st
        self.sem = {}
        names = ["s_t", "s_v", "s_s", "s_cc0", "s_cc1",
                 "s_gd0", "s_gd1", "s_gd2", "s_gd3",
                 "s_w0", "s_w1", "s_sd0", "s_sd1", "s_st0", "s_st1", "s_k0"]
        for s in names:
            self.sem[s] = [st.enter_context(nc.semaphore(s)), 0]
        self._gd_rr = 0
        self._st_rr = 0
        self.T = Stream(self, "tensor")
        self.V = Stream(self, "vector")
        self.S = Stream(self, "scalar")
        self.G = Stream(self, "gpsimd")
        self.Y = Stream(self, "sync")
        self.streams = {"tensor": self.T, "vector": self.V, "scalar": self.S,
                        "gpsimd": self.G, "sync": self.Y}
        self.abf = Arena(st.enter_context(nc.sbuf_tensor("abf", [128, abf_cols], BF)), abf_cols)
        self.af32 = Arena(st.enter_context(nc.sbuf_tensor("af32", [128, af32_cols], F32)), af32_cols)
        self.pbig = [st.enter_context(nc.psum_tensor(f"pb{i}", [128, 512], F32))
                     for i in range(8)]
        self.acc = self.pbig[4:8]
        self.big_free = [None] * 4
        self._big_rr = 0

    def val(self, name):
        return self.sem[name][1]

    def take_big(self):
        i = self._big_rr % 4
        self._big_rr += 1
        fa = self.big_free[i]
        if fa is not None:
            self.T.wait(fa[0], fa[1])
        self.big_free[i] = None
        return i, self.pbig[i]

    def release_big(self, i, semname):
        self.big_free[i] = (semname, self.val(semname))

    def emit_epilogue_sem_reset(self):
        for name in self.sem:
            self.G.wait(name)
        nums = [self.sem[n][0].num for n in self.sem]
        def clr(g, nums=tuple(nums)):
            g.dma_reset(range(min(nums), max(nums) + 1))
            g.sem_clear(range(min(nums), max(nums) + 1))
        self.G.ops.append(clr)

    def finalize(self, block):
        for name in ENGN:
            s = self.streams[name]
            if not s.ops:
                continue
            def body(e, ops=tuple(s.ops)):
                for fn in ops:
                    fn(e)
            getattr(block, name)(body)

def tap(p, name, sbuf_ap):
    if name not in DEBUG_TAPS:
        return
    shape = list(sbuf_ap.shape)
    t = p.nc.dram_tensor("dbg_" + name, shape, sbuf_ap.tensor.dtype,
                         kind="ExternalOutput").ap()
    for sname in ("s_v", "s_s", "s_t", "s_w0", "s_w1", "s_gd0", "s_gd1",
                  "s_gd2", "s_gd3", "s_k0"):
        p.Y.wait(sname)
    ch, cnt = dma(p, p.Y, t, sbuf_ap, st_chan(p))
    for stream in (p.V, p.S, p.T, p.G):
        stream.wait(ch, cnt)


def rope_tile2(p, x0, x1, C, S_, t0, t1):
    vopw(p, lambda v: (v.tensor_tensor(t0, x0, C, ALU.mult),
                       v.tensor_tensor(t1, x1, S_, ALU.mult))[-1])
    vopw(p, lambda v: (v.tensor_tensor(t1, t0, t1, ALU.subtract),
                       v.tensor_tensor(t0, x0, S_, ALU.mult),
                       v.tensor_tensor(x1, x1, C, ALU.mult))[-1])
    vopw(p, lambda v: v.tensor_tensor(x1, t0, x1, ALU.add))
    vopw(p, lambda v: v.tensor_copy(x0, t1))


def vopw(p, fn):
    p.V.op(fn, inc="s_v")
    p.V.wait("s_v")


def dma(p, stream, dst, src, chan, slow=False):
    stream.wait(chan)
    stream.op(lambda e, dst=dst, src=src, slow=slow: e.dma_start(
        out=dst, in_=src, allow_slow_non_contiguous=slow), inc=chan, inc_n=16)
    return chan, p.val(chan)

def gd_chan(p):
    c = f"s_gd{p._gd_rr % 4}"
    p._gd_rr += 1
    return c

def st_chan(p):
    c = f"s_st{p._st_rr % 2}"
    p._st_rr += 1
    return c


def emit_load_consts(p, scr, c_aps):
    for name, buf in [("ones_col", scr["ones_col_bf"]),
                      ("ones_row", scr["ones_row_bf"]),
                      ("neg_ones_row", scr["negones_row_bf"]),
                      ("cblk", scr["cblk_bf"]), ("cblkT", scr["cblkT_bf"])]:
        ch, cnt = dma(p, p.G, buf, c_aps[name], "s_k0")
    for s in (p.V, p.S, p.T):
        s.wait("s_k0", cnt)


def emit_ln(p, x_dram, xn, scr):
    ones_bf = scr["ones_col_bf"]
    onesr_bf = scr["ones_row_bf"]
    sq = scr["ln_sq"]
    stats = scr["ln_stats"]
    tmp32 = scr["t32"]
    statsb = scr["ln_statsb"]
    t_base = p.val("s_t")

    p.G.wait("s_t")
    p.G.wait("s_v")
    ldinfo = []
    for c in range(CH):
        ldinfo.append(dma(p, p.G, xn[:, c * R:(c + 1) * R],
                          x_dram[c * 128:(c + 1) * 128, :], gd_chan(p)))

    for c in range(CH):
        p.V.wait(*ldinfo[c])
        if c >= 2:
            p.V.wait("s_t", t_base + c - 1)
        sl = sq[:, (c % 2) * 1024:(c % 2) * 1024 + 1024]
        p.V.op(lambda v, c=c, sl=sl: v.tensor_tensor(
            sl, xn[:, c * R:(c + 1) * R], xn[:, c * R:(c + 1) * R], ALU.mult),
            inc="s_v")
        p.T.wait("s_v")
        def mms(t, c=c, sl=sl):
            for half in range(2):
                t.matmul(p.acc[half][0:1, :], ones_bf[:, 0:1],
                         xn[:, c * R + half * 512: c * R + half * 512 + 512],
                         start=(c == 0), stop=(c == CH - 1),
                         skip_group_check=True)
            for half in range(2):
                ins = t.matmul(p.acc[2 + half][0:1, :], ones_bf[:, 0:1],
                               sl[:, half * 512:(half + 1) * 512],
                               start=(c == 0), stop=(c == CH - 1),
                               skip_group_check=True)
            return ins
        p.T.op(mms, inc="s_t")

    tap(p, "xb0", xn[:, 0:1024])
    p.V.wait("s_t")
    def stats1(v):
        for half in range(2):
            v.tensor_scalar(stats[0:1, half * 512:(half + 1) * 512],
                            p.acc[half][0:1, :], 1.0 / D, None, ALU.mult)
            v.tensor_scalar(stats[0:1, 1024 + half * 512:1024 + (half + 1) * 512],
                            p.acc[2 + half][0:1, :], 1.0 / D, None, ALU.mult)
        v.tensor_tensor(tmp32[0:1, 0:1024], stats[0:1, 0:1024],
                        stats[0:1, 0:1024], ALU.mult)
        v.tensor_tensor(stats[0:1, 1024:2048], stats[0:1, 1024:2048],
                        tmp32[0:1, 0:1024], ALU.subtract)
        return v.tensor_scalar(stats[0:1, 1024:2048], stats[0:1, 1024:2048],
                               1.0, LN_EPS, ALU.mult, ALU.add)
    p.V.op(stats1, inc="s_v")

    p.S.wait("s_v")
    p.S.op(lambda s: s.activation(tmp32[0:1, 0:1024], stats[0:1, 1024:2048],
                                  AF.Ln), inc="s_s")
    p.V.wait("s_s")
    p.V.op(lambda v: v.tensor_scalar(tmp32[0:1, 0:1024], tmp32[0:1, 0:1024],
                                     -0.5, None, ALU.mult), inc="s_v")
    p.S.wait("s_v")
    p.S.op(lambda s: s.activation(stats[0:1, 1024:2048], tmp32[0:1, 0:1024],
                                  AF.Exp), inc="s_s")

    p.V.wait("s_s")
    def stats2(v):
        v.tensor_tensor(tmp32[0:1, 0:1024], stats[0:1, 0:1024],
                        stats[0:1, 1024:2048], ALU.mult)
        v.tensor_scalar(tmp32[0:1, 0:1024], tmp32[0:1, 0:1024],
                        -1.0, None, ALU.mult)
        v.tensor_copy(statsb[0:1, 0:1024], stats[0:1, 1024:2048])
        return v.tensor_copy(statsb[0:1, 1024:2048], tmp32[0:1, 0:1024])
    p.V.op(stats2, inc="s_v")

    tap(p, "ln_stats", stats[0:1, 0:2048])
    tap(p, "ln_statsb", statsb[0:1, 0:2048])
    banks = [p.take_big() for _ in range(4)]
    p.T.wait("s_v")
    def bcast(t):
        for idx in range(2):
            t.matmul(banks[idx][1][:, :], onesr_bf[0:1, :],
                     statsb[0:1, idx * 512:(idx + 1) * 512], start=True, stop=True)
        for idx in range(2):
            ins = t.matmul(banks[2 + idx][1][:, :], onesr_bf[0:1, :],
                           statsb[0:1, 1024 + idx * 512:1024 + (idx + 1) * 512],
                           start=True, stop=True)
        return ins
    p.T.op(bcast, inc="s_t")

    p.V.wait("s_t")
    def apply(v):
        for c in range(CH):
            for half in range(2):
                sl = slice(c * R + half * 512, c * R + half * 512 + 512)
                v.tensor_tensor(xn[:, sl], xn[:, sl], banks[half][1][:, :], ALU.mult)
                ins = v.tensor_tensor(xn[:, sl], xn[:, sl], banks[2 + half][1][:, :], ALU.add)
        return ins
    p.V.op(apply, inc="s_v")
    for bi, _ in banks:
        p.release_big(bi, "s_v")
    p.T.wait("s_v")


def emit_linear_fm(p, xn, w_src, n_in, n_out, evac, wbufs, ksec_chunks,
                   rows=R, order=None):
    nk = n_in // 128
    nsec = nk // ksec_chunks
    nn = rows // 512
    if order is None:
        order = list(range(n_out // 128))

    def load_sec(slot, m, s):
        src = w_src(m * 128, s * ksec_chunks * 128, ksec_chunks)
        src = src.rearrange("(k p) m -> p k m", p=128)
        ddst = wbufs[slot][:, 0:ksec_chunks * 128].rearrange(
            "p (k m) -> p k m", m=128)
        eng = p.G if src.tensor.dtype == F32 else p.Y
        return dma(p, eng, ddst, src, f"s_w{slot}")

    seq = [(mi, s) for mi in range(len(order)) for s in range(nsec)]
    loaded = {}
    probe = w_src(order[0] * 128, 0, ksec_chunks)
    ldeng = p.G if probe.tensor.dtype == F32 else p.Y
    ldeng.wait("s_t")
    for j in range(min(2, len(seq))):
        mi, s = seq[j]
        loaded[j] = load_sec(j % 2, order[mi], s)

    seq_banks = None
    for j, (mi, s) in enumerate(seq):
        m = order[mi]
        slot = j % 2
        p.T.wait(*loaded[j])
        if s == 0:
            seq_banks = [p.take_big() for _ in range(nn)]
        banks = seq_banks
        for n in range(nn):
            bi, ps = banks[n]
            def mmsec(t, ps=ps, slot=slot, n=n, s=s):
                for kk in range(ksec_chunks):
                    ins = t.matmul(
                        ps[:, :], wbufs[slot][:, kk * 128:(kk + 1) * 128],
                        xn[:, (s * ksec_chunks + kk) * rows + n * 512:
                           (s * ksec_chunks + kk) * rows + n * 512 + 512],
                        start=(s == 0 and kk == 0),
                        stop=(s == nsec - 1 and kk == ksec_chunks - 1))
                return ins
            p.T.op(mmsec, inc=("s_t" if n == nn - 1 else None))
        tdone = p.val("s_t")
        if j + 2 < len(seq):
            nmi, ns = seq[j + 2]
            src_probe = w_src(order[nmi] * 128, ns * ksec_chunks * 128, ksec_chunks)
            e = p.G if src_probe.tensor.dtype == F32 else p.Y
            e.wait("s_t", tdone)
            loaded[j + 2] = load_sec(slot, order[nmi], ns)
        if s == nsec - 1:
            relsem = evac(m, banks)
            for bi, _ in banks:
                p.release_big(bi, relsem)


def make_res_evac(p, x_in_dram, x_out_dram, scr, scale, scale_ap=None,
                  bias_ap=None):
    state_r = {"st": None}
    def evac(m, banks):
        xr = scr["xres"]
        if state_r["st"] is not None:
            p.Y.wait(*state_r["st"])
        xch, xw = dma(p, p.Y, xr, x_in_dram[m * 128:(m + 1) * 128, :], "s_sd0")
        p.S.wait("s_t")
        p.S.wait("s_v")
        for n, (bi, ps) in enumerate(banks):
            sc = scale if scale_ap is None else scale_ap[:, m:m + 1]
            if bias_ap is None:
                p.S.op(lambda s, ps=ps, n=n, sc=sc: s.activation(
                    scr["t32"][:, n * 512:(n + 1) * 512], ps[:, :], AF.Copy,
                    bias=0.0, scale=sc),
                    inc=("s_s" if n == len(banks) - 1 else None))
            else:
                assert scale == 1.0 and scale_ap is None
                p.S.op(lambda s, ps=ps, n=n, m=m: s.add(
                    scr["t32"][:, n * 512:(n + 1) * 512], ps[:, :],
                    bias_ap[:, m:m + 1]),
                    inc=("s_s" if n == len(banks) - 1 else None))
        p.V.wait("s_s")
        p.V.wait(xch, xw)
        p.V.op(lambda v: v.tensor_tensor(xr[:, :], scr["t32"][:, 0:1024],
                                         xr[:, :], ALU.add), inc="s_v")
        p.Y.wait("s_v")
        state_r["st"] = dma(p, p.Y, x_out_dram[m * 128:(m + 1) * 128, :], xr,
                            "s_st0")
        return "s_s"
    return evac



def emit_quant_share(p, src_f32, n_rows, n_cols, out_dram, scr):
    cblk, cblkT = scr["cblk_bf"], scr["cblkT_bf"]
    wt = scr["q_wt"]; wa = scr["q_wa"]; rb = scr["q_rb"]
    m1 = scr["q_m1"]; m2 = scr["q_m2"]; wh = scr["q_wh"]
    rec = scr["q_rec"]; sc2 = scr["q_sc"]; scb = scr["q_scb"]
    state_q = {"st": None}
    W = min(512, n_cols)
    wt = wt[:, 0:W]; wa = wa[:, 0:W]; rb = rb[:, 0:W]
    m1 = m1[:, 0:W]; m2 = m2[:, 0:W]; wh = wh[:, 0:W]
    rec = rec[:, 0:W]; sc2 = sc2[:, 0:W]; scb = scb[:, 0:W]
    for ri in range(n_rows // 128):
        for ci in range(n_cols // W):
            r0, c0 = ri * 128, ci * W
            p.G.wait("s_v")
            wch, wcnt = dma(p, p.G, wt[:, :], src_f32[r0:r0 + 128, c0:c0 + W],
                            "s_gd0")
            p.S.wait(wch, wcnt)
            p.S.wait("s_t")
            p.S.op(lambda s: s.activation(wa[:, :], wt[:, :], AF.Abs), inc="s_s")
            gi, gps = p.take_big()
            bi, bps = p.take_big()
            p.T.wait("s_s")
            p.T.op(lambda t, gps=gps: t.matmul(gps[0:2, 0:W], cblk[:, 0:2], wa[:, :],
                                               start=True, stop=True), inc="s_t")
            p.V.wait("s_t")
            def scale_ops(v, gps=gps):
                v.tensor_scalar(sc2[0:2, :], gps[0:2, 0:W], 1.0 / 64.0, 1e-8,
                                ALU.mult, ALU.max)
                return v.tensor_copy(scb[0:2, :], sc2[0:2, :])
            p.V.op(scale_ops, inc="s_v")
            p.release_big(gi, "s_v")
            p.T.wait("s_v")
            p.T.op(lambda t, bps=bps: t.matmul(bps[:, 0:W], cblkT[0:2, :],
                                               scb[0:2, :], start=True, stop=True),
                   inc="s_t")
            p.V.wait("s_t")
            if state_q["st"] is not None:
                p.V.wait(*state_q["st"])
            def qops(v, bps=bps):
                v.reciprocal(rec[:, :], bps[:, 0:W])
                v.tensor_tensor(rb[:, :], wt[:, :], rec[:, :], ALU.mult)
                v.tensor_scalar(m1[:, :], rb[:, :], 0.5, None, ALU.is_gt)
                v.tensor_scalar(m2[:, :], rb[:, :], -0.5, None, ALU.is_lt)
                v.tensor_tensor(m1[:, :], m1[:, :], m2[:, :], ALU.subtract)
                v.tensor_tensor(wh[:, :], m1[:, :], bps[:, 0:W], ALU.mult)
                v.tensor_tensor(m2[:, :], wh[:, :], wt[:, :], ALU.subtract)
                return v.tensor_tensor(wh[:, :], wt[:, :], m2[:, :], ALU.add)
            p.V.op(qops, inc="s_v")
            p.release_big(bi, "s_v")
            p.Y.wait("s_v")
            state_q["st"] = dma(p, p.Y, out_dram[r0:r0 + 128, c0:c0 + W],
                                wh[:, :], "s_st0")


def emit_quant_ag(p, wgu_sh, wpj_sh, scr, blobs):
    emit_quant_share(p, wgu_sh, 2048, 2048, blobs["sh_gu2d"], scr)
    emit_quant_share(p, wpj_sh, 8192, 256, blobs["sh_pj2d"], scr)
    p.G.wait("s_st0")
    p.G.wait("s_st1")
    p.G.op(lambda g: g.collective_compute(
        "AllGather", ALU.bypass, replica_groups=[list(range(NC))],
        ins=[blobs["sh_gu"]], outs=[blobs["gu_full"]]), inc="s_cc0")
    p.G.op(lambda g: g.collective_compute(
        "AllGather", ALU.bypass, replica_groups=[list(range(NC))],
        ins=[blobs["sh_pj"]], outs=[blobs["pj_full"]]), inc="s_cc0")



def emit_l0ffn(p, x_in, x_out, blobs, scr):
    p.abf.mark()
    p.af32.mark()
    xn = p.abf.alloc(CH * R)
    tmp2048 = p.abf.alloc(2048)
    scr["ln_sq"] = tmp2048
    emit_ln(p, x_in, xn, scr)
    h = p.abf.alloc(64 * R)
    wb0 = p.abf.alloc(2048)
    wb1 = p.abf.alloc(2048)
    gsil = tmp2048[:, 0:1024]
    hsq = tmp2048[:, 1024:2048]
    rstats = p.af32.alloc(1024)

    gu3 = blobs["gu_full"].rearrange("r (d m) -> r d m", m=2048)
    pj3 = blobs["pj_full"].rearrange("r (h m) -> r h m", m=256)

    def gu_src(m0, k0, nkc):
        r, mm = divmod(m0, 2048)
        return gu3[r, k0:k0 + nkc * 128, mm:mm + 128]

    def pj_src(m0, k0, nkc):
        r, mm = divmod(m0, 256)
        return pj3[r, k0:k0 + nkc * 128, mm:mm + 128]

    p.T.wait("s_cc0")
    p.Y.wait("s_cc0")
    p.G.wait("s_cc0")

    order = []
    for g in range(64):
        order.append(g)
        order.append(64 + g)
    state = {"c": 0}

    def evac_gu(m, banks):
        is_gate = m < 64
        g = m if is_gate else m - 64
        p.S.wait("s_t")
        if is_gate:
            p.S.wait("s_v")
            for n, (bi, ps) in enumerate(banks):
                p.S.op(lambda s, ps=ps, n=n: s.activation(
                    gsil[:, n * 512:(n + 1) * 512], ps[:, :], AF.Silu),
                    inc=("s_s" if n == 1 else None))
            return "s_s"
        p.V.wait("s_t")
        p.V.wait("s_s")
        for n, (bi, ps) in enumerate(banks):
            p.V.op(lambda v, ps=ps, n=n, g=g: v.tensor_tensor(
                h[:, g * R + n * 512:g * R + n * 512 + 512],
                gsil[:, n * 512:(n + 1) * 512], ps[:, :], ALU.mult),
                inc=("s_v" if n == 1 else None))
        c = state["c"]
        if c >= 1:
            p.V.wait("s_t")
        p.V.op(lambda v, g=g: v.tensor_tensor(
            hsq[:, :], h[:, g * R:(g + 1) * R], h[:, g * R:(g + 1) * R], ALU.mult),
            inc="s_v")
        p.T.wait("s_v")
        def mm(t, c=c):
            for half in range(2):
                ins = t.matmul(p.acc[half][0:1, :],
                               scr["ones_col_bf"][:, 0:1],
                               hsq[:, half * 512:(half + 1) * 512],
                               start=(c == 0), stop=(c == 63),
                               skip_group_check=True)
            return ins
        p.T.op(mm, inc="s_t")
        state["c"] += 1
        return "s_v"

    emit_linear_fm(p, xn, gu_src, 2048, 16384, evac_gu, [wb0, wb1],
                   ksec_chunks=16, order=order)
    tap(p, "h_pre", h[:, 0:2048])

    p.V.wait("s_t")
    def rmsv(v):
        for half in range(2):
            ins = v.tensor_scalar(rstats[0:1, half * 512:(half + 1) * 512],
                                  p.acc[half][0:1, :],
                                  1.0 / 8192.0, RMS_EPS, ALU.mult, ALU.add)
        return ins
    p.V.op(rmsv, inc="s_v")
    p.S.wait("s_v")
    p.S.op(lambda s: s.activation(scr["t32"][0:1, 0:1024], rstats[0:1, 0:1024],
                                  AF.Ln), inc="s_s")
    p.V.wait("s_s")
    p.V.op(lambda v: v.tensor_scalar(scr["t32"][0:1, 0:1024],
                                     scr["t32"][0:1, 0:1024],
                                     -0.5, None, ALU.mult), inc="s_v")
    p.S.wait("s_v")
    p.S.op(lambda s: s.activation(rstats[0:1, 0:1024], scr["t32"][0:1, 0:1024],
                                  AF.Exp), inc="s_s")
    p.V.wait("s_s")
    p.V.op(lambda v: v.tensor_copy(scr["ln_statsb"][0:1, 0:1024],
                                   rstats[0:1, 0:1024]), inc="s_v")
    ibanks = [p.take_big() for _ in range(2)]
    p.T.wait("s_v")
    def bc(t):
        for idx in range(2):
            ins = t.matmul(ibanks[idx][1][:, :], scr["ones_row_bf"][0:1, :],
                           scr["ln_statsb"][0:1, idx * 512:(idx + 1) * 512],
                           start=True, stop=True)
        return ins
    p.T.op(bc, inc="s_t")
    p.V.wait("s_t")
    def apply_rms(v):
        for g in range(64):
            for half in range(2):
                sl = slice(g * R + half * 512, g * R + half * 512 + 512)
                ins = v.tensor_tensor(h[:, sl], h[:, sl],
                                      ibanks[half][1][:, :], ALU.mult)
        return ins
    p.V.op(apply_rms, inc="s_v")
    for bi, _ in ibanks:
        p.release_big(bi, "s_v")

    tap(p, "h_post", h[:, 0:2048])
    res_evac = make_res_evac(p, x_in, x_out, scr, 0.5)
    p.T.wait("s_v")
    emit_linear_fm(p, h, pj_src, 8192, 2048, res_evac, [wb0, wb1],
                   ksec_chunks=16)
    p.abf.reset()
    p.af32.reset()




def emit_l1ffn(p, x_in, x_out, waps, scr):
    p.abf.mark()
    p.af32.mark()
    xn = p.abf.alloc(CH * R)
    tmp2048 = p.abf.alloc(2048)
    scr["ln_sq"] = tmp2048
    emit_ln(p, x_in, xn, scr)
    NI = 44
    h = p.abf.alloc(NI * R)
    wb0 = p.abf.alloc(2048)
    wb1 = p.abf.alloc(2048)
    gsil = tmp2048[:, 0:1024]
    utmp = tmp2048[:, 1024:2048]
    scbuf = p.af32.alloc(176)
    gs = scbuf[:, 0:44]; us = scbuf[:, 44:88]
    ds = scbuf[:, 88:104]; dsh = scbuf[:, 104:120]
    ch, cnt = dma(p, p.G, gs, waps["gs"], "s_k0")
    ch, cnt = dma(p, p.G, us, waps["us"], "s_k0")
    ch, cnt = dma(p, p.G, ds, waps["ds"], "s_k0")
    p.V.wait("s_k0", cnt)
    p.V.op(lambda v: v.tensor_scalar(dsh[:, :], ds[:, :], 0.5, None, ALU.mult),
           inc="s_v")
    p.S.wait("s_v")

    order = []
    for g in range(NI):
        order.append(g)
        order.append(NI + g)

    def w_src(m0, k0, nkc):
        m = m0 // 128
        if m < NI:
            return waps["gwT"][k0:k0 + nkc * 128, m * 128:(m + 1) * 128]
        return waps["uwT"][k0:k0 + nkc * 128, (m - NI) * 128:(m - NI + 1) * 128]

    def evac_gu(m, banks):
        is_gate = m < NI
        g = m if is_gate else m - NI
        if is_gate:
            p.S.wait("s_t")
            p.S.wait("s_v")
            for n, (bi, ps) in enumerate(banks):
                p.S.op(lambda s, ps=ps, n=n, g=g: s.activation(
                    gsil[:, n * 512:(n + 1) * 512], ps[:, :], AF.Silu,
                    bias=0.0, scale=gs[:, g:g + 1]),
                    inc=("s_s" if n == 1 else None))
            return "s_s"
        p.V.wait("s_t")
        p.V.wait("s_s")
        for n, (bi, ps) in enumerate(banks):
            p.V.op(lambda v, ps=ps, n=n, g=g: v.tensor_scalar(
                utmp[:, n * 512:(n + 1) * 512], ps[:, :], us[:, g:g + 1],
                None, ALU.mult), inc=("s_v" if n == 1 else None))
        p.V.op(lambda v, g=g: v.tensor_tensor(
            h[:, g * R:(g + 1) * R], gsil[:, 0:1024], utmp[:, 0:1024],
            ALU.mult), inc="s_v")
        return "s_v"

    emit_linear_fm(p, xn, w_src, 2048, 2 * NI * 128, evac_gu, [wb0, wb1],
                   ksec_chunks=16, order=order)

    def dw_src(m0, k0, nkc):
        return waps["dwT"][k0:k0 + nkc * 128, m0:m0 + 128]

    res_evac = make_res_evac(p, x_in, x_out, scr, 0.5, scale_ap=dsh)
    p.T.wait("s_v")
    emit_linear_fm(p, h, dw_src, NI * 128, 2048, res_evac, [wb0, wb1],
                   ksec_chunks=11)
    p.abf.reset()
    p.af32.reset()


def build_launch_l0attn():
    nc = bass.Bass("TRN2", target_bir_lowering=False)
    xT = nc.dram_tensor("xT", [D, R], F32, kind="ExternalInput").ap()
    c_aps = {n: nc.dram_tensor(n, s, F32, kind="ExternalInput").ap()
             for n, s in CONST_SHAPES.items()}
    waps = {}
    for n, s in [("Wq", [2048, 2048]), ("Wl", [2048, 1024]),
                 ("Wk", [1024, 256]), ("Wv", [1024, 256]),
                 ("Wq2kv", [128, 64]), ("Wkv2h", [64, 128]),
                 ("Wo", [2048, 2048]), ("b0q", [128, 16]), ("b0l", [128, 8]),
                 ("b0k", [1, 256]), ("b0v", [1, 256]), ("b0k_col", [128, 2]),
                 ("b0kv2h", [128, 1]), ("b0o", [128, 16])]:
        waps[n] = nc.dram_tensor(n, s, F32, kind="ExternalInput").ap()
    waps["mask0"] = nc.dram_tensor("mask0", [128, 4096], BF,
                                   kind="ExternalInput").ap()
    yT = nc.dram_tensor("yT", [D, R], F32, kind="ExternalOutput").ap()
    ag_in = nc.dram_tensor("ag0_in", [1, P0], BF).ap()
    ag_out = nc.dram_tensor("ag0_out", [NC, P0], BF).ap()
    with ExitStack() as st:
        p = Prog(nc, st)
        scr = common_scratch(p)
        block = st.enter_context(nc.Block())
        emit_load_consts(p, scr, c_aps)
        emit_l0attn(p, xT, yT, waps, scr, ag_in, ag_out)
        for chn in ("s_st0", "s_st1", "s_sd0"):
            p.Y.wait(chn)
        p.emit_epilogue_sem_reset()
        p.finalize(block)
    return nc


def build_launch_l1ffn():
    nc = bass.Bass("TRN2", target_bir_lowering=False)
    xT = nc.dram_tensor("xT", [D, R], F32, kind="ExternalInput").ap()
    c_aps = {n: nc.dram_tensor(n, s, F32, kind="ExternalInput").ap()
             for n, s in CONST_SHAPES.items()}
    waps = {
        "gwT": nc.dram_tensor("gwT", [2048, 5632], F32, kind="ExternalInput").ap(),
        "uwT": nc.dram_tensor("uwT", [2048, 5632], F32, kind="ExternalInput").ap(),
        "dwT": nc.dram_tensor("dwT", [5632, 2048], F32, kind="ExternalInput").ap(),
        "gs": nc.dram_tensor("gs", [128, 44], F32, kind="ExternalInput").ap(),
        "us": nc.dram_tensor("us", [128, 44], F32, kind="ExternalInput").ap(),
        "ds": nc.dram_tensor("ds", [128, 16], F32, kind="ExternalInput").ap(),
    }
    yT = nc.dram_tensor("yT", [D, R], F32, kind="ExternalOutput").ap()
    with ExitStack() as st:
        p = Prog(nc, st)
        scr = common_scratch(p)
        block = st.enter_context(nc.Block())
        emit_load_consts(p, scr, c_aps)
        emit_l1ffn(p, xT, yT, waps, scr)
        for chn in ("s_st0", "s_st1", "s_sd0"):
            p.Y.wait(chn)
        p.emit_epilogue_sem_reset()
        p.finalize(block)
    return nc



P0_K = 256 * R
P0_V = R * 260
P0 = P0_K + P0_V + 2048

def _flat2d(ap_row, off, pdim, c):
    return ap_row[0:1, off:off + pdim * c].rearrange("o (p c) -> (o p) c", c=c)

def emit_l0attn(p, x_in, x_out, waps, scr, ag_in, ag_out):
    q2map = {}
    for i, h in enumerate([0, 1, 2, 3, 8, 9, 10, 11]):
        q2map[h] = (i, 0)
    for i, h in enumerate([4, 5, 6, 7, 12, 13, 14, 15]):
        q2map[h] = (i, 64)

    p.abf.mark()
    p.af32.mark()
    q2T = [p.abf.alloc(1024) for _ in range(8)]
    ctx2 = p.abf.alloc(CH * R)
    rowsb = p.abf.alloc(272)
    biases = p.af32.alloc(48)
    b0q_sb = biases[:, 0:16]; b0l_sb = biases[:, 16:24]
    b0k_col = biases[:, 24:26]; b0kv2h_sb = biases[:, 26:27]
    b0o_sb = biases[:, 27:43]
    Bcol = p.af32.alloc(16)
    brow = p.af32.alloc(48)
    kmaxrow = p.af32.alloc(48)
    lbs = p.af32.alloc(256)
    for name, dst in [("b0q", b0q_sb), ("b0l", b0l_sb), ("b0k_col", b0k_col),
                      ("b0kv2h", b0kv2h_sb), ("b0o", b0o_sb)]:
        ch, cnt = dma(p, p.G, dst, waps[name], "s_k0")
    p.S.wait("s_k0", cnt)

    p.abf.mark()
    p.af32.mark()
    xn = p.abf.alloc(CH * R)
    scr["ln_sq"] = p.abf.alloc(2048)
    emit_ln(p, x_in, xn, scr)
    lat = p.abf.alloc(8 * R)
    wb0 = p.abf.alloc(2048)
    wb1 = p.abf.alloc(2048)
    wkv = p.abf.alloc(4096)
    w2kv = p.abf.alloc(64)
    kchunk = p.abf.alloc(1024)
    vrow = p.abf.alloc(260)
    qf = p.abf.alloc(1024)
    smallb = p.abf.alloc(512)
    kvbias = p.af32.alloc(512)

    ch, cnt = dma(p, p.G, smallb[0:1, 0:256], waps["b0k"], "s_k0")
    ch, cnt = dma(p, p.G, smallb[0:1, 256:512], waps["b0v"], "s_k0")
    ch, cnt = dma(p, p.G, w2kv[:, :], waps["Wq2kv"], "s_k0")
    bb = [p.take_big() for _ in range(2)]
    p.T.wait("s_k0", cnt)
    def bkb(t):
        t.matmul(bb[0][1][:, 0:256], scr["ones_row_bf"][0:1, :],
                 smallb[0:1, 0:256], start=True, stop=True)
        return t.matmul(bb[1][1][:, 0:256], scr["ones_row_bf"][0:1, :],
                        smallb[0:1, 256:512], start=True, stop=True)
    p.T.op(bkb, inc="s_t")
    p.S.wait("s_t")
    p.S.op(lambda s: s.copy(kvbias[:, 0:256], bb[0][1][:, 0:256]))
    p.S.op(lambda s: s.copy(kvbias[:, 256:512], bb[1][1][:, 0:256]), inc="s_s")
    for bix, _ in bb:
        p.release_big(bix, "s_s")

    def evac_lat(m, banks):
        p.S.wait("s_t")
        for n, (bix, ps) in enumerate(banks):
            p.S.op(lambda s, ps=ps, n=n, m=m: s.activation(
                lat[:, m * R + n * 512:m * R + n * 512 + 512], ps[:, :],
                AF.Silu, bias=b0l_sb[:, m:m + 1]),
                inc=("s_s" if n == 1 else None))
        return "s_s"
    emit_linear_fm(p, xn, lambda m0, k0, nkc: waps["Wl"][k0:k0 + nkc * 128,
                                                         m0:m0 + 128],
                   2048, 1024, evac_lat, [wb0, wb1], ksec_chunks=16)

    p.G.wait("s_t")
    for kc in range(8):
        dma(p, p.G, wkv[:, kc * 256:(kc + 1) * 256],
            waps["Wk"][kc * 128:(kc + 1) * 128, :], "s_w0" if kc % 2 == 0 else "s_w1")
    for kc in range(8):
        dma(p, p.G, wkv[:, 2048 + kc * 256:2048 + (kc + 1) * 256],
            waps["Wv"][kc * 128:(kc + 1) * 128, :], "s_w0" if kc % 2 == 0 else "s_w1")
    p.T.wait("s_w0"); p.T.wait("s_w1")

    p.T.wait("s_s")
    tap(p, "wkv", wkv)
    tap(p, "lat0", lat[:, 0:2048])
    for half in range(2):
        bi0, ps0 = p.take_big()
        bi1, ps1 = p.take_big()
        def kmm(t, ps0=ps0, ps1=ps1, half=half):
            ins = None
            for kc in range(8):
                for n in range(2):
                    ins = t.matmul((ps0 if n == 0 else ps1)[:, :],
                                   wkv[:, kc * 256 + half * 128:
                                       kc * 256 + half * 128 + 128],
                                   lat[:, kc * R + n * 512:kc * R + n * 512 + 512],
                                   start=(kc == 0), stop=(kc == 7),
                                   skip_group_check=True)
            return ins
        p.T.op(kmm, inc="s_t")
        p.S.wait("s_t")
        p.S.wait("s_v")
        for n, ps in enumerate([ps0, ps1]):
            p.S.op(lambda s, ps=ps, n=n, half=half: s.add(
                kchunk[:, n * 512:(n + 1) * 512], ps[:, :],
                b0k_col[:, half:half + 1]),
                inc=("s_s" if n == 1 else None))
        p.Y.wait("s_s")
        if half == 0:
            tap(p, "kch0", kchunk[:, 0:1024])
        stc, stn = dma(p, p.Y, _flat2d(ag_in, half * 131072, 128, 1024),
                       kchunk[:, 0:1024], st_chan(p))
        p.V.wait(stc, stn)
        p.V.op(lambda v: v.tensor_tensor(kchunk[:, :], kchunk[:, :],
                                         kchunk[:, :], ALU.mult), inc="s_v")
        p.release_big(bi0, "s_s")
        p.release_big(bi1, "s_s")
        for sub in range(2):
            base = 64 * sub
            bn0, pn0 = p.take_big()
            bn1, pn1 = p.take_big()
            p.T.wait("s_v")
            def knm(t, pn0=pn0, pn1=pn1, base=base):
                t.matmul(pn0[0:1, :], scr["ones_col_bf"][base:base + 64, 0:1],
                         kchunk[base:base + 64, 0:512], start=True, stop=True)
                return t.matmul(pn1[0:1, :],
                                scr["ones_col_bf"][base:base + 64, 0:1],
                                kchunk[base:base + 64, 512:1024],
                                start=True, stop=True)
            p.T.op(knm, inc="s_t")
            p.V.wait("s_t")
            kv = 2 * half + sub
            def kred(v, pn0=pn0, pn1=pn1, kv=kv):
                v.reduce_max(brow[0:1, 32:33], pn0[0:1, :], axis=AX.X)
                v.reduce_max(brow[0:1, 33:34], pn1[0:1, :], axis=AX.X)
                return v.tensor_tensor(kmaxrow[0:1, 32 + kv:33 + kv],
                                       brow[0:1, 32:33], brow[0:1, 33:34],
                                       ALU.max)
            p.V.op(kred, inc="s_v")
            p.release_big(bn0, "s_v")
            p.release_big(bn1, "s_v")
    p.V.op(lambda v: v.memset(smallb[:, 0:16], 0.0))
    p.V.op(lambda v: v.tensor_copy(smallb[0:1, 0:4], kmaxrow[0:1, 32:36]),
           inc="s_v")
    p.Y.wait("s_v")
    dma(p, p.Y, _flat2d(ag_in, P0_K + P0_V, 128, 16), smallb[:, 0:16],
        st_chan(p))

    for rt in range(RT):
        bv_, psv = p.take_big()
        def vmm(t, psv=psv, rt=rt):
            ins = None
            for kc in range(8):
                ins = t.matmul(psv[:, 0:256],
                               lat[:, kc * R + rt * 128:kc * R + rt * 128 + 128],
                               wkv[:, 2048 + kc * 256:2048 + (kc + 1) * 256],
                               start=(kc == 0), stop=(kc == 7))
            return ins
        p.T.op(vmm, inc="s_t")
        p.V.wait("s_t")
        if rt > 0:
            p.V.wait(*vst)
        p.V.op(lambda v: v.memset(vrow[:, :], 1.0))
        def vev(v, psv=psv):
            return v.tensor_tensor(
                vrow[:, :].rearrange("p (g c) -> p g c", c=65)[:, :, 0:64],
                psv[:, 0:256].rearrange("p (g c) -> p g c", c=64),
                kvbias[:, 256:512].rearrange("p (g c) -> p g c", c=64),
                ALU.add)
        p.V.op(vev, inc="s_v")
        p.release_big(bv_, "s_v")
        p.Y.wait("s_v")
        vst = dma(p, p.Y, _flat2d(ag_in, P0_K + rt * (128 * 260), 128, 260),
                  vrow[:, :], st_chan(p))

    if "agi" in DEBUG_TAPS:
        agi = p.nc.dram_tensor("dbg_agi", [1, 131072], BF,
                               kind="ExternalOutput").ap()
        p.Y.wait("s_st0"); p.Y.wait("s_st1"); p.Y.wait("s_sd0")
        dma(p, p.Y, agi, ag_in[0:1, 0:131072], "s_st1")
    p.G.wait("s_st0"); p.G.wait("s_st1")
    p.G.op(lambda g: g.collective_compute(
        "AllGather", ALU.bypass, replica_groups=[list(range(NC))],
        ins=[ag_in], outs=[ag_out]), inc="s_cc1")

    def evac_q(m, banks):
        h = m
        ti, base = q2map[h]
        p.S.wait("s_t")
        p.S.wait("s_v")
        for n, (bix, ps) in enumerate(banks):
            p.S.op(lambda s, ps=ps, n=n, h=h: s.add(
                qf[:, n * 512:(n + 1) * 512], ps[:, :],
                b0q_sb[:, h:h + 1]), inc=("s_s" if n == 1 else None))
        q2b = [p.take_big() for _ in range(2)]
        p.T.wait("s_s")
        def q2mm(t):
            ins = None
            for n in range(2):
                ins = t.matmul(q2b[n][1][0:64, :], w2kv[:, :],
                               qf[:, n * 512:(n + 1) * 512],
                               start=True, stop=True)
            return ins
        p.T.op(q2mm, inc="s_t")
        p.S.wait("s_t")
        for n in range(2):
            p.S.op(lambda s, n=n, ti=ti, base=base: s.activation(
                q2T[ti][base:base + 64, n * 512:(n + 1) * 512],
                q2b[n][1][0:64, :], AF.Copy, bias=0.0, scale=0.125),
                inc=("s_s" if n == 1 else None))
        for bix, _ in q2b:
            p.release_big(bix, "s_s")
        p.V.wait("s_s")
        def qsq2(v, ti=ti, base=base):
            return v.tensor_tensor(qf[0:64, 0:1024], q2T[ti][base:base + 64, :],
                                   q2T[ti][base:base + 64, :], ALU.mult)
        p.V.op(qsq2, inc="s_v")
        qn = [p.take_big() for _ in range(2)]
        p.T.wait("s_v")
        def qnm(t):
            t.matmul(qn[0][1][0:1, :], scr["ones_col_bf"][0:64, 0:1],
                     qf[0:64, 0:512], start=True, stop=True)
            return t.matmul(qn[1][1][0:1, :], scr["ones_col_bf"][0:64, 0:1],
                            qf[0:64, 512:1024], start=True, stop=True)
        p.T.op(qnm, inc="s_t")
        p.V.wait("s_t")
        def qred(v, h=h):
            v.reduce_max(brow[0:1, 32:33], qn[0][1][0:1, :], axis=AX.X)
            v.reduce_max(brow[0:1, 33:34], qn[1][1][0:1, :], axis=AX.X)
            return v.tensor_tensor(brow[0:1, h:h + 1], brow[0:1, 32:33],
                                   brow[0:1, 33:34], ALU.max)
        p.V.op(qred, inc="s_v")
        for bix, _ in qn:
            p.release_big(bix, "s_v")
        return "s_s"

    emit_linear_fm(p, xn, lambda m0, k0, nkc: waps["Wq"][k0:k0 + nkc * 128,
                                                         m0:m0 + 128],
                   2048, 2048, evac_q, [wb0, wb1], ksec_chunks=16)

    p.abf.reset()
    p.af32.reset()

    p.abf.mark()
    p.af32.mark()
    kT = [p.abf.alloc(8192), p.abf.alloc(8192)]
    vbuf = p.abf.alloc(64 * 260)
    PT = p.abf.alloc(64 * 256)
    mask = p.abf.alloc(16 * 256)
    ctxh = p.abf.alloc(1024)
    wkv2h = p.abf.alloc(128)
    for sname in ("s_t", "s_v", "s_s"):
        p.G.wait(sname)
    ch, cnt = dma(p, p.G, mask, waps["mask0"], "s_k0")
    ch, cnt = dma(p, p.G, wkv2h[0:64, :], waps["Wkv2h"], "s_k0")

    p.G.wait("s_cc1")
    if "agk" in DEBUG_TAPS:
        agk = p.nc.dram_tensor("dbg_agk", [NC, 131072], BF,
                               kind="ExternalOutput").ap()
        p.Y.wait("s_cc1")
        dma(p, p.Y, agk, ag_out[:, 0:131072], "s_st1")
    asm = []
    for t in range(2):
        asm.append(dma(p, p.G,
                       kT[t][:, :].rearrange("p (r c) -> p r c", c=1024),
                       ag_out[:, t * 131072:(t + 1) * 131072].rearrange(
                           "r (p c) -> p r c", c=1024), gd_chan(p)))
    for r in range(NC):
        asm.append(dma(p, p.G,
                       vbuf[:, 2080 * r:2080 * (r + 1)].rearrange(
                           "p (l c) -> p l c", c=260),
                       ag_out[r:r + 1, P0_K:P0_K + P0_V].rearrange(
                           "o (l p c) -> (o p) l c", l=8, c=260), gd_chan(p)))
    p.Y.wait("s_cc1")
    for r in range(NC):
        ch2, cnt2 = dma(p, p.Y, rowsb[0:1, 16 + 4 * r:20 + 4 * r],
                        ag_out[r:r + 1, P0_K + P0_V:P0_K + P0_V + 4], "s_sd1")
    for a in asm:
        p.Y.wait(*a)
    tap(p, "kT0_early", kT[0][:, 0:4096])
    p.V.wait("s_sd1", cnt2)
    def km(v):
        ins = None
        for g in range(4):
            gr = rowsb[0:1, 16:48].rearrange("o (r g) -> o g r", g=4)
            v.tensor_copy(kmaxrow[0:1, 0:8], gr[:, g, :])
            ins = v.reduce_max(kmaxrow[0:1, 32 + g:33 + g],
                               kmaxrow[0:1, 0:8], axis=AX.X)
        return ins
    p.V.op(km, inc="s_v")
    def prod(v):
        ins = None
        for h in range(16):
            ins = v.tensor_tensor(brow[0:1, 16 + h:17 + h],
                                  brow[0:1, h:h + 1],
                                  kmaxrow[0:1, 32 + h // 4:33 + h // 4],
                                  ALU.mult)
        return ins
    p.V.op(prod, inc="s_v")
    p.S.wait("s_v")
    p.S.op(lambda s: s.activation(brow[0:1, 32:48], brow[0:1, 16:32], AF.Ln),
           inc="s_s")
    p.V.wait("s_s")
    p.V.op(lambda v: v.tensor_scalar(brow[0:1, 32:48], brow[0:1, 32:48],
                                     0.5, None, ALU.mult), inc="s_v")
    p.S.wait("s_v")
    p.S.op(lambda s: s.activation(brow[0:1, 16:32], brow[0:1, 32:48], AF.Exp),
           inc="s_s")
    p.V.wait("s_s")
    p.V.op(lambda v: v.tensor_copy(rowsb[0:1, 0:16], brow[0:1, 16:32]),
           inc="s_v")
    bB = p.take_big()
    p.T.wait("s_v")
    p.T.op(lambda t: t.matmul(bB[1][:, 0:16], scr["negones_row_bf"][0:1, :],
                              rowsb[0:1, 0:16], start=True, stop=True),
           inc="s_t")
    p.S.wait("s_t")
    p.S.op(lambda s: s.copy(Bcol[:, :], bB[1][:, 0:16]), inc="s_s")
    p.release_big(bB[0], "s_s")

    for i in range(4):
        fa = p.big_free[i]
        if fa is not None:
            p.T.wait(fa[0], fa[1])
        p.big_free[i] = None
    ctx_ps = p.pbig[0]
    st_ps = [p.pbig[1], p.pbig[2]]
    lb_ps = p.pbig[3]
    p.G.wait("s_k0")
    for a in asm:
        p.T.wait(*a)
        p.S.wait(*a)
        p.V.wait(*a)

    exp_done = p.val("s_s")
    pv_done = p.val("s_t")
    evq_done = p.val("s_v")
    for h in range(16):
        g = h // 4
        ti, base = q2map[h]
        kt = kT[g // 2]
        for J in range(4):
            kcs = 16 * (J + 1)
            exp_cnt = []
            for kc in range(kcs):
                jj, r = kc // 8, kc % 8
                col = 1024 * r + 128 * jj
                slot = kc % 2
                if kc >= 2:
                    p.T.wait("s_s", exp_cnt[kc - 2])
                else:
                    p.T.wait("s_s", exp_done)
                    p.T.wait("s_v", evq_done)
                def stm(t, kt=kt, col=col, slot=slot, ti=ti, base=base, J=J):
                    return t.matmul(st_ps[slot][:, 0:256],
                                    kt[base:base + 64, col:col + 128],
                                    q2T[ti][base:base + 64,
                                            J * 256:J * 256 + 256],
                                    start=True, stop=True)
                p.T.op(stm, inc="s_t")
                p.S.wait("s_t")
                def expm(s, slot=slot, kc=kc, h=h):
                    return s.activation(PT[:, kc * 256:(kc + 1) * 256],
                                        st_ps[slot][:, 0:256], AF.Exp,
                                        bias=Bcol[:, h:h + 1])
                p.S.op(expm, inc="s_s")
                exp_cnt.append(p.val("s_s"))
            p.V.wait("s_s", exp_cnt[-1])
            def mk(v, J=J, kcs=kcs):
                return v.tensor_tensor(PT[:, (kcs - 16) * 256:kcs * 256],
                                       PT[:, (kcs - 16) * 256:kcs * 256],
                                       mask[:, :], ALU.mult)
            p.V.op(mk, inc="s_v")
            p.T.wait("s_v")
            def pvm(t, g=g, kcs=kcs):
                ins = None
                for kc in range(kcs):
                    jj, r = kc // 8, kc % 8
                    voff = 2080 * r + 260 * jj
                    ins = t.matmul(ctx_ps[0:65, 0:256],
                                   vbuf[:, voff + 65 * g:voff + 65 * g + 65],
                                   PT[:, kc * 256:(kc + 1) * 256],
                                   start=(kc == 0), stop=(kc == kcs - 1))
                return ins
            p.T.op(pvm, inc="s_t")
            pv_done = p.val("s_t")
            p.V.wait("s_t", pv_done)
            p.V.op(lambda v: v.reciprocal(lbs[0:1, 0:256],
                                          ctx_ps[64:65, 0:256]))
            p.V.op(lambda v: v.tensor_copy(rowsb[0:1, 16:272],
                                           lbs[0:1, 0:256]), inc="s_v")
            p.T.wait("s_v")
            p.T.op(lambda t: t.matmul(lb_ps[0:64, 0:256],
                                      scr["ones_row_bf"][0:1, 0:64],
                                      rowsb[0:1, 16:272], start=True,
                                      stop=True), inc="s_t")
            p.S.wait("s_t")
            p.S.op(lambda s: s.copy(lbs[0:64, 0:256], lb_ps[0:64, 0:256]),
                   inc="s_s")
            p.V.wait("s_s")
            def cev(v, J=J):
                return v.tensor_tensor(ctxh[0:64, J * 256:(J + 1) * 256],
                                       ctx_ps[0:64, 0:256], lbs[0:64, 0:256],
                                       ALU.mult)
            p.V.op(cev, inc="s_v")
            evq_done = p.val("s_v")
            exp_done = p.val("s_s")
        kb = [p.pbig[1], p.pbig[2]]
        p.T.wait("s_v", evq_done)
        def khm(t):
            ins = None
            for n in range(2):
                ins = t.matmul(kb[n][:, 0:512], wkv2h[0:64, :],
                               ctxh[0:64, n * 512:(n + 1) * 512],
                               start=True, stop=True)
            return ins
        p.T.op(khm, inc="s_t")
        p.S.wait("s_t")
        for n in range(2):
            p.S.op(lambda s, n=n, h=h: s.add(
                ctx2[:, h * R + n * 512:h * R + n * 512 + 512],
                kb[n][:, 0:512], b0kv2h_sb[:, 0:1]),
                inc=("s_s" if n == 1 else None))
        exp_done = p.val("s_s")

    tap(p, "ctx2", ctx2)
    tap(p, "q2T0", q2T[0])
    tap(p, "kT0", kT[0][:, 0:4096])
    p.abf.reset()
    p.af32.reset()

    p.abf.mark()
    wb0b = p.abf.alloc(2048)
    wb1b = p.abf.alloc(2048)
    for i in range(4):
        p.big_free[i] = ("s_s", p.val("s_s"))
    res_evac = make_res_evac(p, x_in, x_out, scr, 1.0, bias_ap=b0o_sb)
    p.T.wait("s_s")
    p.T.wait("s_v")
    emit_linear_fm(p, ctx2, lambda m0, k0, nkc: waps["Wo"][k0:k0 + nkc * 128,
                                                           m0:m0 + 128],
                   2048, 2048, res_evac, [wb0b, wb1b], ksec_chunks=16)
    p.abf.reset()



P1_K = 2048 * R
P1_V = R * 2048
P1_R = 64 * R
P1 = P1_K + P1_V + P1_R + 2048
SCALER1 = float(1.0 / np.sqrt(192.0))

def emit_l1attn(p, x_in, x_out, waps, scr, ag_in, ag_out):
    p.abf.mark()
    p.af32.mark()
    ctx2 = p.abf.alloc(CH * R)
    q = p.abf.alloc(CH * R)
    qrope = [p.abf.alloc(1024) for _ in range(8)]
    rowsb = p.abf.alloc(704)
    biases = p.af32.alloc(64)
    b_dq = biases[:, 0:12]; b_uq = biases[:, 12:28]
    b_dkv = biases[:, 28:32]; b_uk = biases[:, 32:48]; b_o = biases[:, 48:64]
    tabs = p.abf.alloc(512)
    tabC = tabs[:, 0:256]; tabS = tabs[:, 256:512]
    bcast = p.af32.alloc(1024)
    lbs = p.af32.alloc(512)
    Bcol = p.af32.alloc(16)
    brow = p.af32.alloc(64)
    kk = None
    for name, dst in [("b1dq", b_dq), ("b1uq", b_uq), ("b1dkv", b_dkv),
                      ("b1uk", b_uk), ("b1o", b_o), ("ropeC_rm", tabC),
                      ("ropeS_rm", tabS)]:
        kk = dma(p, p.G, dst, waps[name], "s_k0")
    p.S.wait(*kk)
    p.V.wait(*kk)
    p.T.wait(*kk)

    p.abf.mark()
    xn = p.abf.alloc(CH * R)
    scr["ln_sq"] = p.abf.alloc(2048)
    emit_ln(p, x_in, xn, scr)

    def mk_fm_evac(dst, bias_sb):
        def evac(m, banks):
            p.S.wait("s_t")
            for n, (bix, ps) in enumerate(banks):
                p.S.op(lambda s, ps=ps, n=n, m=m: s.add(
                    dst[:, m * R + n * 512:m * R + n * 512 + 512], ps[:, :],
                    bias_sb[:, m:m + 1]), inc=("s_s" if n == 1 else None))
            return "s_s"
        return evac

    def build_row_bcast(src_row_ap, width, dstc):
        p.G.wait("s_t")
        kk = dma(p, p.G, rowsb[0:1, 192:192 + width], src_row_ap, "s_k0")
        nb = (width + 511) // 512
        bb = [p.take_big() for _ in range(nb)]
        p.T.wait(*kk)
        def bc(t):
            ins = None
            for n in range(nb):
                w = min(512, width - n * 512)
                ins = t.matmul(bb[n][1][:, 0:w], scr["ones_row_bf"][0:1, :],
                               rowsb[0:1, 192 + n * 512:192 + n * 512 + w],
                               start=True, stop=True)
            return ins
        p.T.op(bc, inc="s_t")
        p.S.wait("s_t")
        p.S.wait("s_v")
        for n in range(nb):
            w = min(512, width - n * 512)
            p.S.op(lambda s, n=n, w=w: s.copy(
                bcast[:, dstc + n * 512:dstc + n * 512 + w], bb[n][1][:, 0:w]),
                inc=("s_s" if n == nb - 1 else None))
        for bix, _ in bb:
            p.release_big(bix, "s_s")

    p.abf.mark()
    ckv = p.abf.alloc(4 * R)
    wb0 = p.abf.alloc(2048)
    wb1 = p.abf.alloc(2048)
    kstage = p.abf.alloc(1024)
    vstage = p.abf.alloc(2048)
    krot_rm = p.abf.alloc(512)
    krfm = p.abf.alloc(1024)
    ksq = p.abf.alloc(1024)
    wrk = p.abf.alloc(1024)
    wuv = p.abf.alloc(8192)
    smallb = p.abf.alloc(160)

    emit_linear_fm(p, xn, lambda m0, k0, nkc: waps["W_dkv"][k0:k0 + nkc * 128,
                                                            m0:m0 + 128],
                   2048, 512, mk_fm_evac(ckv, b_dkv), [wb0, wb1], ksec_chunks=16)

    kk = dma(p, p.G, wrk[:, :].rearrange("p (k m) -> p k m", m=64),
             waps["W_rkp"].rearrange("(k p) m -> p k m", p=128), "s_w0")
    kk2 = dma(p, p.G, wuv[:, :].rearrange("p (k m) -> p k m", m=2048),
              waps["W_uv"].rearrange("(k p) m -> p k m", p=128), "s_w1")
    build_row_bcast(waps["b1rk_row"], 64, 0)

    p.T.wait(*kk)
    for rt in range(RT):
        bk_, psk = p.take_big()
        def krm(t, psk=psk, rt=rt):
            ins = None
            for kc in range(CH):
                ins = t.matmul(psk[:, 0:64],
                               xn[:, kc * R + rt * 128:kc * R + rt * 128 + 128],
                               wrk[:, kc * 64:(kc + 1) * 64],
                               start=(kc == 0), stop=(kc == CH - 1))
            return ins
        p.T.op(krm, inc="s_t")
        p.V.wait("s_t")
        p.V.wait("s_s")
        p.V.op(lambda v, psk=psk, rt=rt: v.tensor_tensor(
            krot_rm[:, rt * 64:(rt + 1) * 64], psk[:, 0:64], bcast[:, 0:64],
            ALU.add), inc="s_v")
        p.release_big(bk_, "s_v")
    p.V.wait("s_s")
    for rt in range(RT):
        rope_tile2(p, krot_rm[:, rt * 64:rt * 64 + 32],
                  krot_rm[:, rt * 64 + 32:rt * 64 + 64],
                  tabC[:, rt * 32:(rt + 1) * 32],
                  tabS[:, rt * 32:(rt + 1) * 32],
                  ksq[:, 0:32], ksq[:, 32:64])
    def tkrot(v):
        ins = None
        for rt in range(RT):
            for a in range(4):
                for c in range(2):
                    ins = v.transpose(
                        krfm[32 * c:32 * c + 32,
                             rt * 128 + 32 * a:rt * 128 + 32 * a + 32],
                        krot_rm[32 * a:32 * a + 32,
                                rt * 64 + 32 * c:rt * 64 + 32 * c + 32])
        return ins
    p.V.op(tkrot, inc="s_v")
    p.V.wait("s_v")
    p.Y.wait("s_v")
    krst = dma(p, p.Y, _flat2d(ag_in, P1_K + P1_V, 64, 1024), krfm[0:64, :],
               st_chan(p))

    p.V.wait(*krst)
    p.V.op(lambda v: v.tensor_tensor(ksq[0:64, :], krfm[0:64, :],
                                     krfm[0:64, :], ALU.mult), inc="s_v")
    bkr0, pkr0 = p.take_big()
    bkr1, pkr1 = p.take_big()
    p.T.wait("s_v")
    def krn(t):
        t.matmul(pkr0[0:1, :], scr["ones_col_bf"][0:64, 0:1],
                 ksq[0:64, 0:512], start=True, stop=True)
        return t.matmul(pkr1[0:1, :], scr["ones_col_bf"][0:64, 0:1],
                        ksq[0:64, 512:1024], start=True, stop=True)
    p.T.op(krn, inc="s_t")
    p.V.wait("s_t")
    p.V.op(lambda v: v.tensor_copy(bcast[0:1, 0:512], pkr0[0:1, :]))
    p.V.op(lambda v: v.tensor_copy(bcast[0:1, 512:1024], pkr1[0:1, :]),
           inc="s_v")
    p.release_big(bkr0, "s_v")
    p.release_big(bkr1, "s_v")

    def evac_k(m, banks):
        h = m
        p.S.wait("s_t")
        p.S.wait("s_v")
        for n, (bix, ps) in enumerate(banks):
            p.S.op(lambda s, ps=ps, n=n, h=h: s.add(
                kstage[:, n * 512:(n + 1) * 512], ps[:, :],
                b_uk[:, h:h + 1]), inc=("s_s" if n == 1 else None))
        p.Y.wait("s_s")
        stc, stn = dma(p, p.Y, _flat2d(ag_in, h * 131072, 128, 1024),
                       kstage[:, 0:1024], st_chan(p))
        p.V.wait(stc, stn)
        p.V.op(lambda v: v.tensor_tensor(ksq[:, :], kstage[:, :],
                                         kstage[:, :], ALU.mult), inc="s_v")
        kb0, pk0 = p.take_big()
        kb1, pk1 = p.take_big()
        p.T.wait("s_v")
        def knm(t):
            t.matmul(pk0[0:1, :], scr["ones_col_bf"][:, 0:1], ksq[:, 0:512],
                     start=True, stop=True)
            return t.matmul(pk1[0:1, :], scr["ones_col_bf"][:, 0:1],
                            ksq[:, 512:1024], start=True, stop=True)
        p.T.op(knm, inc="s_t")
        p.V.wait("s_t")
        vopw(p, lambda v: (v.tensor_tensor(ksq[0:1, 0:512], pk0[0:1, :],
                                           bcast[0:1, 0:512], ALU.add),
                           v.tensor_tensor(ksq[0:1, 512:1024], pk1[0:1, :],
                                           bcast[0:1, 512:1024], ALU.add))[-1])
        vopw(p, lambda v: (v.reduce_max(brow[0:1, 48:49], ksq[0:1, 0:512],
                                        axis=AX.X),
                           v.reduce_max(brow[0:1, 49:50], ksq[0:1, 512:1024],
                                        axis=AX.X))[-1])
        vopw(p, lambda v, h=h: v.tensor_tensor(brow[0:1, 32 + h:33 + h],
                                               brow[0:1, 48:49],
                                               brow[0:1, 49:50], ALU.max))
        p.release_big(kb0, "s_v")
        p.release_big(kb1, "s_v")
        return "s_s"

    emit_linear_fm(p, ckv, lambda m0, k0, nkc: waps["W_uk"][k0:k0 + nkc * 128,
                                                            m0:m0 + 128],
                   512, 2048, evac_k, [wb0, wb1], ksec_chunks=4)

    p.T.wait(*kk2)
    for rt in range(RT):
        for mc in range(4):
            build_row_bcast(waps["b1uv_row"][0:1, mc * 512:(mc + 1) * 512],
                            512, 0)
            bv_, psv = p.take_big()
            def vmm(t, psv=psv, rt=rt, mc=mc):
                ins = None
                for kc in range(4):
                    ins = t.matmul(
                        psv[:, :],
                        ckv[:, kc * R + rt * 128:kc * R + rt * 128 + 128],
                        wuv[:, kc * 2048 + mc * 512:kc * 2048 + mc * 512 + 512],
                        start=(kc == 0), stop=(kc == 3))
                return ins
            p.T.op(vmm, inc="s_t")
            p.V.wait("s_t")
            p.V.wait("s_s")
            if rt > 0 and mc == 0:
                p.V.wait(*vst)
            p.V.op(lambda v, psv=psv, mc=mc: v.tensor_tensor(
                vstage[:, mc * 512:(mc + 1) * 512], psv[:, :],
                bcast[:, 0:512], ALU.add), inc="s_v")
            p.release_big(bv_, "s_v")
        p.Y.wait("s_v")
        vst = dma(p, p.Y, _flat2d(ag_in, P1_K + rt * 128 * 2048, 128, 2048),
                  vstage[:, :], st_chan(p))

    p.V.op(lambda v: v.memset(smallb[:, 0:16], 0.0))
    p.V.op(lambda v: v.tensor_copy(smallb[0:1, 0:16], brow[0:1, 32:48]),
           inc="s_v")
    p.Y.wait("s_v")
    dma(p, p.Y, _flat2d(ag_in, P1_K + P1_V + P1_R, 128, 16), smallb[:, 0:16],
        st_chan(p))

    p.G.wait("s_st0"); p.G.wait("s_st1")
    p.G.op(lambda g: g.collective_compute(
        "AllGather", ALU.bypass, replica_groups=[list(range(NC))],
        ins=[ag_in], outs=[ag_out]), inc="s_cc1")
    p.abf.reset()

    p.abf.mark()
    cq = p.abf.alloc(12 * R)
    wb0 = p.abf.alloc(2048)
    wb1 = p.abf.alloc(2048)
    qrot_rm = p.abf.alloc(4096)
    wrq = p.abf.alloc(3072)
    qsq = p.abf.alloc(1024)

    emit_linear_fm(p, xn, lambda m0, k0, nkc: waps["W_dq"][k0:k0 + nkc * 128,
                                                           m0:m0 + 128],
                   2048, 1536, mk_fm_evac(cq, b_dq), [wb0, wb1], ksec_chunks=16)
    emit_linear_fm(p, cq, lambda m0, k0, nkc: waps["W_uq"][k0:k0 + nkc * 128,
                                                           m0:m0 + 128],
                   1536, 2048, mk_fm_evac(q, b_uq), [wb0, wb1], ksec_chunks=12)

    for quar in range(4):
        build_row_bcast(waps["b1rq_row"][0:1, quar * 256:(quar + 1) * 256],
                        256, 0)
        p.G.wait("s_t")
        kk = dma(p, p.G, wrq[:, :].rearrange("p (k m) -> p k m", m=256),
                 waps["W_rqp"][:, quar * 256:(quar + 1) * 256].rearrange(
                     "(k p) m -> p k m", p=128), "s_w0")
        p.T.wait(*kk)
        for rt in range(RT):
            bq_, psq = p.take_big()
            def qrm(t, psq=psq, rt=rt):
                ins = None
                for kc in range(12):
                    ins = t.matmul(
                        psq[:, 0:256],
                        cq[:, kc * R + rt * 128:kc * R + rt * 128 + 128],
                        wrq[:, kc * 256:(kc + 1) * 256],
                        start=(kc == 0), stop=(kc == 11))
                return ins
            p.T.op(qrm, inc="s_t")
            p.V.wait("s_t")
            p.V.wait("s_s")
            p.V.op(lambda v, psq=psq, rt=rt: v.tensor_tensor(
                qrot_rm[:, rt * 256:(rt + 1) * 256], psq[:, 0:256],
                bcast[:, 0:256], ALU.add), inc="s_v")
            p.release_big(bq_, "s_v")
        if quar == 0:
            tap(p, "qrot_pre", qrot_rm[:, 0:2048])
        for rt in range(RT):
            for hh in range(4):
                base_c = rt * 256 + hh * 64
                tslot = 64 * ((rt * 4 + hh) % 4)
                rope_tile2(p, qrot_rm[:, base_c:base_c + 32],
                           qrot_rm[:, base_c + 32:base_c + 64],
                           tabC[:, rt * 32:(rt + 1) * 32],
                           tabS[:, rt * 32:(rt + 1) * 32],
                           qsq[:, tslot:tslot + 32],
                           qsq[:, tslot + 32:tslot + 64])
        if quar == 0:
            tap(p, "qrot_post", qrot_rm[:, 0:2048])
        p.V.wait("s_v")
        def tq(v, quar=quar):
            ins = None
            for rt in range(RT):
                for hh in range(4):
                    h = quar * 4 + hh
                    ti, base = h // 2, 64 * (h % 2)
                    for a in range(4):
                        for c in range(2):
                            ins = v.transpose(
                                qrope[ti][base + 32 * c:base + 32 * c + 32,
                                          rt * 128 + 32 * a:rt * 128 + 32 * a + 32],
                                qrot_rm[32 * a:32 * a + 32,
                                        rt * 256 + hh * 64 + 32 * c:
                                        rt * 256 + hh * 64 + 32 * c + 32])
            return ins
        p.V.op(tq, inc="s_v")
        p.V.wait("s_v")
        p.T.wait("s_v")

    for h in range(16):
        ti, base = h // 2, 64 * (h % 2)
        p.V.wait("s_t")
        p.V.wait("s_s")
        p.V.op(lambda v, h=h: v.tensor_tensor(
            qsq[:, :], q[:, h * R:(h + 1) * R], q[:, h * R:(h + 1) * R],
            ALU.mult), inc="s_v")
        qb0, pq0 = p.take_big()
        qb1, pq1 = p.take_big()
        p.T.wait("s_v")
        def qn1(t):
            t.matmul(pq0[0:1, :], scr["ones_col_bf"][:, 0:1], qsq[:, 0:512],
                     start=True, stop=False, skip_group_check=True)
            return t.matmul(pq1[0:1, :], scr["ones_col_bf"][:, 0:1],
                            qsq[:, 512:1024], start=True, stop=False,
                            skip_group_check=True)
        p.T.op(qn1, inc="s_t")
        p.V.wait("s_t")
        p.V.op(lambda v, ti=ti, base=base: v.tensor_tensor(
            qsq[base:base + 64, 0:1024], qrope[ti][base:base + 64, :],
            qrope[ti][base:base + 64, :], ALU.mult), inc="s_v")
        p.T.wait("s_v")
        def qn2(t, base=base):
            t.matmul(pq0[0:1, :], scr["ones_col_bf"][base:base + 64, 0:1],
                     qsq[base:base + 64, 0:512], start=False, stop=True,
                     skip_group_check=True)
            return t.matmul(pq1[0:1, :],
                            scr["ones_col_bf"][base:base + 64, 0:1],
                            qsq[base:base + 64, 512:1024], start=False,
                            stop=True, skip_group_check=True)
        p.T.op(qn2, inc="s_t")
        p.V.wait("s_t")
        vopw(p, lambda v: (v.reduce_max(brow[0:1, 48:49], pq0[0:1, :],
                                        axis=AX.X),
                           v.reduce_max(brow[0:1, 49:50], pq1[0:1, :],
                                        axis=AX.X))[-1])
        vopw(p, lambda v, h=h: v.tensor_tensor(brow[0:1, h:h + 1],
                                               brow[0:1, 48:49],
                                               brow[0:1, 49:50], ALU.max))
        p.release_big(qb0, "s_v")
        p.release_big(qb1, "s_v")
    p.abf.reset()
    p.abf.reset()

    p.abf.mark()
    kTh = p.abf.alloc(8192)
    vh = p.abf.alloc(8192)
    PT = p.abf.alloc(16 * 512)
    krotT = p.abf.alloc(8192)
    for sname in ("s_t", "s_v", "s_s"):
        p.G.wait(sname)
        p.Y.wait(sname)

    p.G.wait("s_cc1")
    asm = []
    for half in range(2):
        asm.append(dma(
            p, p.G,
            krotT[64 * half:64 * half + 64, :].rearrange(
                "p (r c) -> p r c", c=1024),
            ag_out[:, P1_K + P1_V:P1_K + P1_V + P1_R].rearrange(
                "r (p c) -> p r c", c=1024), gd_chan(p)))
    p.Y.wait("s_cc1")
    kk = None
    for r in range(NC):
        kk = dma(p, p.Y, rowsb[0:1, 16 + 16 * r:32 + 16 * r],
                 ag_out[r:r + 1, P1_K + P1_V + P1_R:P1_K + P1_V + P1_R + 16],
                 "s_sd1")
    p.V.wait(*kk)
    gr = rowsb[0:1, 16:144].rearrange("o (r h) -> o h r", h=16)
    for h in range(16):
        vopw(p, lambda v, h=h: v.tensor_copy(brow[0:1, 48:56], gr[:, h, :]))
        vopw(p, lambda v, h=h: v.reduce_max(brow[0:1, 32 + h:33 + h],
                                            brow[0:1, 48:56], axis=AX.X))
    def prod(v):
        ins = None
        for h in range(16):
            ins = v.tensor_tensor(brow[0:1, 16 + h:17 + h], brow[0:1, h:h + 1],
                                  brow[0:1, 32 + h:33 + h], ALU.mult)
        return ins
    p.V.op(prod, inc="s_v")
    p.V.wait("s_v")
    p.S.wait("s_v")
    p.S.op(lambda s: s.activation(brow[0:1, 48:64], brow[0:1, 16:32], AF.Ln),
           inc="s_s")
    p.V.wait("s_s")
    p.V.op(lambda v: v.tensor_scalar(brow[0:1, 48:64], brow[0:1, 48:64],
                                     0.5, None, ALU.mult), inc="s_v")
    p.S.wait("s_v")
    p.S.op(lambda s: s.activation(brow[0:1, 16:32], brow[0:1, 48:64], AF.Exp),
           inc="s_s")
    p.V.wait("s_s")
    vopw(p, lambda v: v.tensor_scalar(brow[0:1, 16:32], brow[0:1, 16:32],
                                      SCALER1, None, ALU.mult))
    vopw(p, lambda v: v.tensor_copy(rowsb[0:1, 0:16], brow[0:1, 16:32]))
    bB = p.take_big()
    p.T.wait("s_v")
    p.T.op(lambda t: t.matmul(bB[1][:, 0:16], scr["negones_row_bf"][0:1, :],
                              rowsb[0:1, 0:16], start=True, stop=True),
           inc="s_t")
    p.S.wait("s_t")
    p.S.op(lambda s: s.copy(Bcol[:, :], bB[1][:, 0:16]), inc="s_s")
    p.release_big(bB[0], "s_s")

    for i in range(4):
        fa = p.big_free[i]
        if fa is not None:
            p.T.wait(fa[0], fa[1])
        p.big_free[i] = None
    ctx_ps = p.acc[0]
    l_ps = p.acc[1]
    st_ps = [p.pbig[0], p.pbig[1]]
    lb_ps = p.pbig[2]
    for a in asm:
        p.T.wait(*a)

    tap(p, "tabs1", tabs)
    tap(p, "Bcol1", Bcol)
    tap(p, "q0", q[:, 0:1024])
    tap(p, "qrope0", qrope[0])
    tap(p, "krotT1", krotT[:, 0:2048])
    exp_done = p.val("s_s")
    evq_done = p.val("s_v")
    for h in range(16):
        ti, base = h // 2, 64 * (h % 2)
        p.G.wait("s_t")
        kk = dma(p, p.G, kTh[:, :].rearrange("p (r c) -> p r c", c=1024),
                 ag_out[:, h * 131072:(h + 1) * 131072].rearrange(
                     "r (p c) -> p r c", c=1024), gd_chan(p))
        vls = []
        for r in range(NC):
            vls.append(dma(
                p, p.G,
                vh[:, 1024 * r:1024 * (r + 1)].rearrange(
                    "p (l c) -> p l c", c=128),
                ag_out[r:r + 1, P1_K:P1_K + P1_V].rearrange(
                    "o (l p c) -> (o p) l c", l=8, c=2048)[:, :,
                                                          128 * h:128 * h + 128],
                gd_chan(p)))
        p.T.wait(*kk)
        for a in vls:
            p.T.wait(*a)
        for qs in range(2):
            for quar in range(4):
                ex0 = exp_done
                for kci in range(16):
                    kc = quar * 16 + kci
                    slot = kci % 2
                    if kci >= 2:
                        p.T.wait("s_s", exp_cnt[kci - 2])
                    else:
                        p.T.wait("s_s", ex0)
                        p.T.wait("s_v", evq_done)
                    if kci == 0:
                        exp_cnt = []
                    def stm(t, kc=kc, slot=slot, h=h, ti=ti, base=base, qs=qs):
                        t.matmul(st_ps[slot][:, :],
                                 kTh[:, 128 * kc:128 * kc + 128],
                                 q[:, h * R + qs * 512:h * R + qs * 512 + 512],
                                 start=True, stop=False)
                        return t.matmul(
                            st_ps[slot][:, :],
                            krotT[base:base + 64, 128 * kc:128 * kc + 128],
                            qrope[ti][base:base + 64,
                                      qs * 512:qs * 512 + 512],
                            start=False, stop=True)
                    p.T.op(stm, inc="s_t")
                    p.S.wait("s_t")
                    def expm(s, slot=slot, kci=kci, h=h):
                        return s.activation(
                            PT[:, kci * 512:(kci + 1) * 512],
                            st_ps[slot][:, :], AF.Exp,
                            bias=Bcol[:, h:h + 1], scale=SCALER1)
                    p.S.op(expm, inc="s_s")
                    exp_cnt.append(p.val("s_s"))
                p.T.wait("s_s", exp_cnt[-1])
                def pvm(t, quar=quar, h=h, qs=qs):
                    ins = None
                    for kci in range(16):
                        kc = quar * 16 + kci
                        first = (quar == 0 and kci == 0)
                        last = (quar == 3 and kci == 15)
                        t.matmul(ctx_ps[:, :], vh[:, 128 * kc:128 * kc + 128],
                                 PT[:, kci * 512:(kci + 1) * 512],
                                 start=first, stop=last,
                                 skip_group_check=True)
                        ins = t.matmul(l_ps[0:1, :],
                                       scr["ones_col_bf"][:, 0:1],
                                       PT[:, kci * 512:(kci + 1) * 512],
                                       start=first, stop=last,
                                       skip_group_check=True)
                    return ins
                p.T.op(pvm, inc="s_t")
                exp_done = p.val("s_s")
            p.V.wait("s_t")
            p.V.op(lambda v: v.reciprocal(lbs[0:1, 0:512], l_ps[0:1, :]))
            p.V.op(lambda v: v.tensor_copy(rowsb[0:1, 192:704],
                                           lbs[0:1, 0:512]), inc="s_v")
            p.T.wait("s_v")
            p.T.op(lambda t: t.matmul(lb_ps[:, :],
                                      scr["ones_row_bf"][0:1, :],
                                      rowsb[0:1, 192:704], start=True,
                                      stop=True), inc="s_t")
            p.S.wait("s_t")
            p.S.op(lambda s: s.copy(lbs[:, 0:512], lb_ps[:, :]), inc="s_s")
            p.V.wait("s_s")
            def cev(v, h=h, qs=qs):
                return v.tensor_tensor(
                    ctx2[:, h * R + qs * 512:h * R + qs * 512 + 512],
                    ctx_ps[:, :], lbs[:, 0:512], ALU.mult)
            p.V.op(cev, inc="s_v")
            evq_done = p.val("s_v")
            exp_done = p.val("s_s")
    tap(p, "ctx2b", ctx2)
    tap(p, "kTh15", kTh[:, 0:2048])
    p.abf.reset()

    p.abf.mark()
    wb0b = p.abf.alloc(2048)
    wb1b = p.abf.alloc(2048)
    for i in range(4):
        p.big_free[i] = ("s_v", p.val("s_v"))
    res_evac = make_res_evac(p, x_in, x_out, scr, 1.0, bias_ap=b_o)
    p.T.wait("s_v")
    p.T.wait("s_s")
    emit_linear_fm(p, ctx2, lambda m0, k0, nkc: waps["W_o1"][k0:k0 + nkc * 128,
                                                             m0:m0 + 128],
                   2048, 2048, res_evac, [wb0b, wb1b], ksec_chunks=16)
    p.abf.reset()
    p.af32.reset()


def build_launch_l1attn():
    nc = bass.Bass("TRN2", target_bir_lowering=False)
    xT = nc.dram_tensor("xT", [D, R], F32, kind="ExternalInput").ap()
    c_aps = {n: nc.dram_tensor(n, s, F32, kind="ExternalInput").ap()
             for n, s in CONST_SHAPES.items()}
    waps = {}
    for n, s in [("W_dkv", [2048, 512]), ("W_dq", [2048, 1536]),
                 ("W_uq", [1536, 2048]), ("W_uk", [512, 2048]),
                 ("W_uv", [512, 2048]), ("W_rqp", [1536, 1024]),
                 ("W_rkp", [2048, 64]), ("W_o1", [2048, 2048]),
                 ("b1dq", [128, 12]), ("b1uq", [128, 16]),
                 ("b1dkv", [128, 4]), ("b1uk", [128, 16]),
                 ("b1o", [128, 16]), ("b1uv_row", [1, 2048]),
                 ("b1rq_row", [1, 1024]), ("b1rk_row", [1, 64]),
                 ("ropeC_rm", [128, 256]), ("ropeS_rm", [128, 256])]:
        waps[n] = nc.dram_tensor(n, s, F32, kind="ExternalInput").ap()
    yT = nc.dram_tensor("yT", [D, R], F32, kind="ExternalOutput").ap()
    ag_in = nc.dram_tensor("ag1_in", [1, P1], BF).ap()
    ag_out = nc.dram_tensor("ag1_out", [NC, P1], BF).ap()
    with ExitStack() as st:
        p = Prog(nc, st)
        scr = common_scratch(p)
        block = st.enter_context(nc.Block())
        emit_load_consts(p, scr, c_aps)
        emit_l1attn(p, xT, yT, waps, scr, ag_in, ag_out)
        for chn in ("s_st0", "s_st1", "s_sd0", "s_sd1"):
            p.Y.wait(chn)
        p.emit_epilogue_sem_reset()
        p.finalize(block)
    return nc



def common_scratch(p):
    scr = {}
    scr["ones_col_bf"] = p.abf.alloc(16)[:, 0:1]
    scr["ones_row_bf"] = p.abf.alloc(128)[0:1, :]
    scr["negones_row_bf"] = p.abf.alloc(128)[0:1, :]
    scr["ln_statsb"] = p.abf.alloc(2048)[0:1, :]
    scr["cblk_bf"] = p.abf.alloc(16)[:, 0:2]
    scr["cblkT_bf"] = p.abf.alloc(128)[0:2, :]
    scr["ln_stats"] = p.af32.alloc(2048)
    scr["t32"] = p.af32.alloc(1024)
    scr["xres"] = p.af32.alloc(1024)
    return scr

def quant_scratch(p, scr):
    scr["q_wt"] = p.abf.alloc(512)
    scr["q_wa"] = p.abf.alloc(512)
    scr["q_rb"] = p.abf.alloc(512)
    scr["q_m1"] = p.abf.alloc(512)
    scr["q_m2"] = p.abf.alloc(512)
    scr["q_wh"] = p.abf.alloc(512)
    scr["q_scb"] = p.abf.alloc(512)[0:2, :]
    scr["q_rec"] = p.af32.alloc(512)
    scr["q_sc"] = p.af32.alloc(512)[0:2, :]


def build_launch_l0ffn(reuse_quant):
    nc = bass.Bass("TRN2", target_bir_lowering=False)
    xT = nc.dram_tensor("xT", [D, R], F32, kind="ExternalInput").ap()
    c_aps = {n: nc.dram_tensor(n, s, F32, kind="ExternalInput").ap()
             for n, s in CONST_SHAPES.items()}
    yT = nc.dram_tensor("yT", [D, R], F32, kind="ExternalOutput").ap()

    if reuse_quant:
        sh_gu = nc.dram_tensor("gu_q_in", [1, 2048 * 2048], BF, kind="ExternalInput").ap()
        sh_pj = nc.dram_tensor("pj_q_in", [1, 8192 * 256], BF, kind="ExternalInput").ap()
        sh_gu_i = nc.dram_tensor("sh_gu", [1, 2048 * 2048], BF).ap()
        sh_pj_i = nc.dram_tensor("sh_pj", [1, 8192 * 256], BF).ap()
    else:
        wgu_sh = nc.dram_tensor("wq_gu_sh", [2048, 2048], F32, kind="ExternalInput").ap()
        wpj_sh = nc.dram_tensor("wq_pj_sh", [8192, 256], F32, kind="ExternalInput").ap()
        sh_gu_i = nc.dram_tensor("sh_gu", [1, 2048 * 2048], BF).ap()
        sh_pj_i = nc.dram_tensor("sh_pj", [1, 8192 * 256], BF).ap()
        gu_out = nc.dram_tensor("gu_q", [1, 2048 * 2048], BF, kind="ExternalOutput").ap()
        pj_out = nc.dram_tensor("pj_q", [1, 8192 * 256], BF, kind="ExternalOutput").ap()
    gu_full = nc.dram_tensor("gu_full", [8, 2048 * 2048], BF).ap()
    pj_full = nc.dram_tensor("pj_full", [8, 8192 * 256], BF).ap()

    blobs = {
        "sh_gu": sh_gu_i, "sh_pj": sh_pj_i,
        "sh_gu2d": sh_gu_i.rearrange("o (a b) -> (o a) b", b=2048),
        "sh_pj2d": sh_pj_i.rearrange("o (a b) -> (o a) b", b=256),
        "gu_full": gu_full, "pj_full": pj_full,
    }

    with ExitStack() as st:
        p = Prog(nc, st)
        scr = common_scratch(p)
        block = st.enter_context(nc.Block())
        emit_load_consts(p, scr, c_aps)
        if reuse_quant:
            dma(p, p.Y, sh_gu_i, sh_gu, "s_st0")
            dma(p, p.Y, sh_pj_i, sh_pj, "s_st0")
            p.G.wait("s_st0")
            p.G.op(lambda g: g.collective_compute(
                "AllGather", ALU.bypass, replica_groups=[list(range(NC))],
                ins=[blobs["sh_gu"]], outs=[blobs["gu_full"]]), inc="s_cc0")
            p.G.op(lambda g: g.collective_compute(
                "AllGather", ALU.bypass, replica_groups=[list(range(NC))],
                ins=[blobs["sh_pj"]], outs=[blobs["pj_full"]]), inc="s_cc0")
        else:
            p.abf.mark(); p.af32.mark()
            quant_scratch(p, scr)
            emit_quant_ag(p, wgu_sh, wpj_sh, scr, blobs)
            dma(p, p.Y, gu_out, sh_gu_i, "s_st1")
            dma(p, p.Y, pj_out, sh_pj_i, "s_st1")
            p.abf.reset(); p.af32.reset()
        if "gu_full_dump" in DEBUG_TAPS:
            gfd = nc.dram_tensor("gu_full_dump", [8, 2048 * 2048], BF,
                                 kind="ExternalOutput").ap()
            p.Y.wait("s_cc0")
            dma(p, p.Y, gfd, gu_full, "s_st1")
        emit_l0ffn(p, xT, yT, blobs, scr)
        for ch in ("s_st0", "s_st1", "s_sd0"):
            p.Y.wait(ch)
        p.emit_epilogue_sem_reset()
        p.finalize(block)
    return nc



def prep_inputs(ins):
    x = np.asarray(ins["L0_attn_x"], np.float32)
    tr = lambda a: np.ascontiguousarray(np.asarray(a, np.float32).T)
    w_guT = tr(ins["L0_ffn_w_gate_up"])
    w_pjT = tr(ins["L0_ffn_w_proj"])
    gwT = tr(ins["L1_ffn_gate_w"])
    uwT = tr(ins["L1_ffn_up_w"])
    dwT = tr(ins["L1_ffn_down_w"])
    gs_c = _bias_col(ins["L1_ffn_gate_s"])
    us_c = _bias_col(ins["L1_ffn_up_s"])
    ds_c = _bias_col(ins["L1_ffn_down_s"])
    pq = _perm_rope_cols(16, 64)
    W_rqp = np.ascontiguousarray(np.asarray(ins["L1_attn_W_rq"], np.float32)[:, pq])
    b1rq_row = np.ascontiguousarray(np.asarray(ins["L1_attn_b_rq"], np.float32)[pq][None, :])
    pk = _perm_rope_cols(1, 64)
    W_rkp = np.ascontiguousarray(np.asarray(ins["L1_attn_W_rk"], np.float32)[:, pk])
    b1rk_row = np.ascontiguousarray(np.asarray(ins["L1_attn_b_rk"], np.float32)[pk][None, :])
    per_core = []
    for c in range(NC):
        d = dict(_consts(c))
        d["xT"] = _shard_xT(x, c)
        d["wq_gu_sh"] = np.ascontiguousarray(w_guT[:, 2048 * c:2048 * (c + 1)])
        d["wq_pj_sh"] = np.ascontiguousarray(w_pjT[:, 256 * c:256 * (c + 1)])
        d["gwT"] = gwT; d["uwT"] = uwT; d["dwT"] = dwT
        d["gs"] = gs_c; d["us"] = us_c; d["ds"] = ds_c
        f32 = lambda a: np.ascontiguousarray(np.asarray(a, np.float32))
        d["Wq"] = f32(ins["L0_attn_Wq"]); d["Wl"] = f32(ins["L0_attn_Wl"])
        d["Wk"] = f32(ins["L0_attn_Wk"]); d["Wv"] = f32(ins["L0_attn_Wv"])
        d["Wq2kv"] = f32(ins["L0_attn_Wq2kv"])
        d["Wkv2h"] = f32(ins["L0_attn_Wkv2h"])
        d["Wo"] = f32(ins["L0_attn_Wo"])
        d["b0q"] = _bias_col(ins["L0_attn_bq"])
        d["b0l"] = _bias_col(ins["L0_attn_bl"])
        d["b0k"] = f32(ins["L0_attn_bk"])[None, :]
        d["b0v"] = f32(ins["L0_attn_bv"])[None, :]
        d["b0k_col"] = _bias_col(ins["L0_attn_bk"])
        d["b0kv2h"] = _bias_col(ins["L0_attn_bkv2h"])
        d["b0o"] = _bias_col(ins["L0_attn_bo"])
        d["mask0"] = _mask0(c)
        d["W_dkv"] = f32(ins["L1_attn_W_dkv"])
        d["W_dq"] = f32(ins["L1_attn_W_dq"])
        d["W_uq"] = f32(ins["L1_attn_W_uq"])
        d["W_uk"] = f32(ins["L1_attn_W_uk"])
        d["W_uv"] = f32(ins["L1_attn_W_uv"])
        d["W_rqp"] = W_rqp
        d["W_rkp"] = W_rkp
        d["W_o1"] = f32(ins["L1_attn_W_o"])
        d["b1dq"] = _bias_col(ins["L1_attn_b_dq"])
        d["b1uq"] = _bias_col(ins["L1_attn_b_uq"])
        d["b1dkv"] = _bias_col(ins["L1_attn_b_dkv"])
        d["b1uk"] = _bias_col(ins["L1_attn_b_uk"])
        d["b1o"] = _bias_col(ins["L1_attn_b_o"])
        d["b1uv_row"] = np.ascontiguousarray(f32(ins["L1_attn_b_uv"])[None, :])
        d["b1rq_row"] = b1rq_row
        d["b1rk_row"] = b1rk_row
        d["ropeC_rm"], d["ropeS_rm"] = _rope_tables_rm(c)
        per_core.append(d)
    return per_core


def kernel(**inputs):
    per_core = prep_inputs(inputs)
    consts = list(CONST_SHAPES)
    ff0q = build_launch_l0ffn(reuse_quant=False)
    at0 = build_launch_l0attn()
    ff0r = build_launch_l0ffn(reuse_quant=True)
    ff1 = build_launch_l1ffn()
    at1 = build_launch_l1attn()

    k_at0 = ["Wq", "Wl", "Wk", "Wv", "Wq2kv", "Wkv2h", "Wo", "b0q", "b0l",
             "b0k", "b0v", "b0k_col", "b0kv2h", "b0o", "mask0"] + consts
    k_ff1 = ["gwT", "uwT", "dwT", "gs", "us", "ds"] + consts
    k_at1 = ["W_dkv", "W_dq", "W_uq", "W_uk", "W_uv", "W_rqp", "W_rkp",
             "W_o1", "b1dq", "b1uq", "b1dkv", "b1uk", "b1o", "b1uv_row",
             "b1rq_row", "b1rk_row", "ropeC_rm", "ropeS_rm"] + consts

    def run(nc, keys, x_shards, extra=None):
        in_maps = []
        for c in range(NC):
            m = {k: per_core[c][k] for k in keys}
            m["xT"] = x_shards[c]
            if extra is not None:
                m.update(extra[c])
            in_maps.append(m)
        return run_bass_kernel_spmd(nc, in_maps, core_ids=list(range(NC)))

    xs = [per_core[c]["xT"] for c in range(NC)]
    r1 = run(ff0q, ["wq_gu_sh", "wq_pj_sh"] + consts, xs)
    xs = [np.asarray(r1.results[c]["yT"]) for c in range(NC)]
    quant = [{"gu_q_in": np.asarray(r1.results[c]["gu_q"]),
              "pj_q_in": np.asarray(r1.results[c]["pj_q"])} for c in range(NC)]
    r2 = run(at0, k_at0, xs)
    xs = [np.asarray(r2.results[c]["yT"]) for c in range(NC)]
    r3 = run(ff0r, consts, xs, extra=quant)
    xs = [np.asarray(r3.results[c]["yT"]) for c in range(NC)]
    r4 = run(ff1, k_ff1, xs)
    xs = [np.asarray(r4.results[c]["yT"]) for c in range(NC)]
    r5 = run(at1, k_at1, xs)
    xs = [np.asarray(r5.results[c]["yT"]) for c in range(NC)]
    r6 = run(ff1, k_ff1, xs)
    ys = [np.asarray(r6.results[c]["yT"]) for c in range(NC)]
    return _unshard_y(ys).astype(np.float32)
